# version 02
# baseline (speedup 1.0000x reference)
import os
import numpy as np
import ml_dtypes
import concourse.bass as bass
import concourse.mybir as mybir
import concourse.library_config as libcfg
from concourse.bass_utils import run_bass_kernel_spmd
from contextlib import ExitStack

F32 = mybir.dt.float32
BF16 = mybir.dt.bfloat16
I16 = mybir.dt.int16
ALU = mybir.AluOpType
ACTF = mybir.ActivationFunctionType
AXX = mybir.AxisListType.X

NCORE = 8
B, T, D = 32, 2048, 1024
SEQ = 4
KD = D // 128
H, HK = 16, 64
CH = 128
NE0, FI0 = 15, 512
KF = FI0 // 128
E1, FF1 = 8, 4096
KFF = FF1 // 128
TPAD = T + 2
SEQ_ORDER = (1, 2, 3, 0)


class Plan:

    def __init__(self, top_idx, combine_weights):
        top = np.asarray(top_idx).astype(np.int64)
        cw = np.asarray(combine_weights, np.float32)
        C1 = top.shape[0]
        pairs = [[[] for _ in range(E1)] for _ in range(NCORE)]
        for e in range(E1):
            for c in range(C1):
                r = int(top[c, e])
                core = r // T
                if 0 <= core < NCORE:
                    pairs[core][e].append(r % T)
        maxn = max(len(pairs[co][e]) for co in range(NCORE) for e in range(E1))
        self.PE_ = max(128, -(-max(maxn, 1) // 128) * 128)
        self.PT = self.PE_ // 128
        self.ZROW = E1 * self.PE_
        self.gidx = np.zeros((NCORE, 32, E1, self.PE_ // 16), np.int16)
        self.wv = np.zeros((NCORE, 128, E1, self.PT), np.float32)
        contrib = [[[] for _ in range(T)] for _ in range(NCORE)]
        for co in range(NCORE):
            for e in range(E1):
                for slot, r in enumerate(pairs[co][e]):
                    self.gidx[co, slot % 16, e, slot // 16] = r
                    self.wv[co, slot % 128, e, slot // 128] = cw[co * T + r, e]
                    contrib[co][r].append(e * self.PE_ + slot)
        self.gidx[:, 16:32] = self.gidx[:, 0:16]
        self.KMAX = max(1, max(len(contrib[co][r])
                               for co in range(NCORE) for r in range(T)))
        self.cidx = np.full((NCORE, 32, self.KMAX, 2, 1024 // 16), self.ZROW,
                            np.int16)
        for co in range(NCORE):
            for r in range(T):
                hf, j = r // 1024, r % 1024
                for k, pr in enumerate(contrib[co][r]):
                    self.cidx[co, j % 16, k, hf, j // 16] = pr
        self.cidx[:, 16:32] = self.cidx[:, 0:16]


class Haz:
    OWN = {"PE": "t", "DV": "v", "AC": "s", "GP": "g"}

    def __init__(self):
        self.seq = []
        self.emitted = {e: [] for e in "tvsgy"}

    @staticmethod
    def _fp(paps):
        out = []
        for p_ in paps:
            if not hasattr(p_, "memref"):
                continue
            ds = str(p_.dtype)
            isz = 4
            if ("16" in ds) or ("fp16" in ds):
                isz = 2
            elif "8" in ds:
                isz = 1
            pairs = list(p_.ap)
            ps, pn = pairs[0]
            if ps > 0:
                p0, foff = divmod(p_.offset, ps)
            else:
                p0, foff = 0, p_.offset
            fhi = foff
            for s_, n_ in pairs[1:]:
                fhi += abs(s_) * (n_ - 1)
            out.append((p_.memref, p0, p0 + pn,
                        foff * isz, fhi * isz + isz))
        return out

    @staticmethod
    def _ov(a, b):
        for m1, pl1, ph1, l1, h1 in a:
            for m2, pl2, ph2, l2, h2 in b:
                if (m1 == m2 and pl1 < ph2 and pl2 < ph1
                        and l1 < h2 and l2 < h1):
                    return m1
        return None

    def rec(self, eng, binst):
        mi = binst.ins
        try:
            self.emitted[eng].append((self._fp(mi.ins), self._fp(mi.outs)))
        except Exception:
            self.emitted[eng].append(([], []))
        return binst

    def replay(self):
        engs = "tvsgy"
        keys = list(engs) + ["WL", "XL"]
        self.clock = {e: {f: 0 for f in keys} for e in engs}
        self.idx = {e: 0 for e in engs}
        self.sem2idx = {s: {} for s in ("PE", "DV", "AC", "GP")}
        self.snaps = {e: [{f: 0 for f in keys}] for e in engs}
        self.fps = {e: [] for e in engs}
        self.dmasnap = {"WL": {}, "XL": {}}
        self.flags = {}
        ptr = {e: 0 for e in engs}
        for item in self.seq:
            if item[0] == "op":
                _, eng, semval, site = item
                reads, writes = self.emitted[eng][ptr[eng]]
                ptr[eng] += 1
                self.idx[eng] += 1
                n = self.idx[eng]
                isdma = semval is not None and semval[0] in self.dmasnap
                if semval is not None and semval[0] in self.sem2idx:
                    self.sem2idx[semval[0]][semval[1]] = n
                if isdma:
                    self.dmasnap[semval[0]][semval[1]] = \
                        dict(self.clock[eng])
                if eng in "vg":
                    lo = max(self.clock[eng][eng], n - 1 - 64)
                    for m in range(lo + 1, n):
                        rm, wm, sm, sv_m = self.fps[eng][m - 1]
                        dm = sv_m is not None and sv_m[0] in self.dmasnap
                        if dm and isdma:
                            continue
                        if dm and sv_m[1] <= self.clock[eng][sv_m[0]]:
                            continue
                        hit = (self._ov(reads, wm) or self._ov(writes, rm)
                               or self._ov(writes, wm))
                        if hit:
                            key = (eng, sm, site, hit)
                            e_ = self.flags.get(key)
                            self.flags[key] = \
                                (e_[0] + 1, min(e_[1], n - m)) if e_ \
                                else (1, n - m)
                self.fps[eng].append((reads, writes, site, semval))
                self.snaps[eng].append(dict(self.clock[eng]))
            else:
                _, eng, sem, val = item
                own = self.OWN.get(sem)
                c = self.clock[eng]
                if own is not None:
                    i = self.sem2idx[sem].get(val)
                    if i is None:
                        ks = [k for k in self.sem2idx[sem] if k <= val]
                        i = self.sem2idx[sem][max(ks)] if ks else 0
                    for k, v in self.snaps[own][i].items():
                        if v > c[k]:
                            c[k] = v
                    if i > c[own]:
                        c[own] = i
                elif sem in self.dmasnap:
                    if val > c[sem]:
                        c[sem] = val
                    snap = self.dmasnap[sem].get(val)
                    if snap is None:
                        ks = [k for k in self.dmasnap[sem] if k <= val]
                        snap = self.dmasnap[sem][max(ks)] if ks else None
                    if snap:
                        for k, v in snap.items():
                            if v > c[k]:
                                c[k] = v

    def report(self):
        self.replay()
        lines = []
        for (eng, sm, sn, memref), (cnt, mind) in sorted(self.flags.items()):
            lines.append(f"HAZ {eng} x{cnt} mindist={mind} {memref}: "
                         f"{sm} -> {sn}")
        return lines


class Prog:
    def __init__(self, nc, st):
        self.nc = nc
        self.st = st
        self.streams = {k: [] for k in "tvsgy"}
        self.sem = {}
        self.cnt = {}
        for name in ("WL", "XL", "PE", "DV", "AC", "GP"):
            self.sem[name] = st.enter_context(nc.semaphore("s" + name))
            self.cnt[name] = 0
        self.bank_free = [("DV", 0)] * 8
        self.haz = Haz() if os.environ.get("MK_HAZ", "") == "1" else None

    def _site(self):
        import traceback
        for fr in reversed(traceback.extract_stack(limit=12)[:-2]):
            if fr.name not in ("op", "dma", "wait", "_site", "emit"):
                return f"{fr.name}:{fr.lineno}"
        return "?"

    def sb(self, name, shape, dtype):
        return self.st.enter_context(
            self.nc.sbuf_tensor(name, shape, dtype)).ap()

    def emit(self, eng, f):
        self.streams[eng].append(f)

    def wait(self, eng, sem, val):
        if val and val > 0:
            if self.haz:
                self.haz.seq.append(("wt", eng, sem, val))
            self.emit(eng, lambda e, s=self.sem[sem], v=val: e.wait_ge(s, v))

    def inc(self, sem, k=1):
        self.cnt[sem] += k
        return self.cnt[sem]

    def op(self, eng, f, sem=None, k=1):
        if sem is None:
            if self.haz:
                self.haz.seq.append(("op", eng, None, self._site()))
                self.emit(eng, lambda e, ff=f, en=eng:
                          self.haz.rec(en, ff(e)))
            else:
                self.emit(eng, f)
            return None
        v = self.inc(sem, k)
        if self.haz:
            self.haz.seq.append(("op", eng, (sem, v), self._site()))
            self.emit(eng, lambda e, ff=f, s=self.sem[sem], kk=k, en=eng:
                      self.haz.rec(en, ff(e).then_inc(s, kk)))
        else:
            self.emit(eng,
                      lambda e, ff=f, s=self.sem[sem], kk=k:
                      ff(e).then_inc(s, kk))
        return v

    def dma(self, eng, out, in_, sem="XL", **kw):
        def f(e, o=out, i=in_, kws=kw):
            return e.dma_start(out=o, in_=i, **kws)
        return self.op(eng, f, sem, 16)

    def bank(self, b, eng="t"):
        s, v = self.bank_free[b]
        if v:
            self.wait(eng, s, v)

    def bank_done(self, b, sem, val):
        self.bank_free[b] = (sem, val)

    def emit_all(self, blk):
        for eng, meth in (("g", blk.gpsimd), ("v", blk.vector),
                          ("s", blk.scalar), ("t", blk.tensor),
                          ("y", blk.sync)):
            fs = self.streams[eng]
            self.streams[eng] = []
            if fs:
                meth(lambda e, fl=fs: [f(e) for f in fl])


def build(plan, stages=("P", "C", "A", "B", "L1"), nseq=SEQ):
    dbg_on = os.environ.get("MK_DBG", "") == "1"
    PE_, PT, KMAX, ZROW = plan.PE_, plan.PT, plan.KMAX, plan.ZROW
    NPG = PE_ // 16
    nc = bass.Bass("TRN2", target_bir_lowering=False,
                   detect_race_conditions=(
                       os.environ.get("MK_RACE", "") == "1"))
    hb = {}

    def hin(name, shape, dtype=F32):
        hb[name] = nc.dram_tensor(name, shape, dtype,
                                  kind="ExternalInput").ap()
        return hb[name]

    hin("x", [SEQ, T, D])
    hin("mix_r", [128, KD]); hin("mix_k", [128, KD])
    hin("mix_v", [128, KD]); hin("mix_g", [128, KD])
    hin("tdec", [1, H]); hin("tfaa", [1, H])
    for nm in ("W_r", "W_k", "W_v", "W_g", "W_o", "Wgt"):
        hin(nm, [D, D])
    hin("ln_w", [1, D]); hin("ln_b", [1, D])
    hin("rt_w", [D, 16]); hin("rt_b", [1, 16])
    hin("eg", [NE0, D, FI0]); hin("eu", [NE0, D, FI0]); hin("ed", [NE0, FI0, D])
    hin("cv", [FI0, D, 3]); hin("su", [D, FI0]); hin("sd", [FI0, D])
    hin("Wlam", [H, D]); hin("blam", [H, 1]); hin("WB", [H, D])
    hin("WC", [D, H]); hin("rmsw", [1, D])
    hin("W1", [E1, D, FF1]); hin("b1l", [128, E1, KFF])
    hin("b2", [E1, 1, D]); hin("W2", [E1, FF1, D])
    hin("l1w", [128, E1, PT])
    hin("l1gidx", [32, E1, NPG], I16)
    hin("l1cidx", [32, KMAX, 2, 64], I16)

    out_h = nc.dram_tensor("out", [SEQ, T, D], F32, kind="ExternalOutput").ap()
    dbg_h = (nc.dram_tensor("dbg", [24, 128, D], F32, kind="ExternalOutput").ap()
             if dbg_on else None)

    def scr(name, shape, dtype):
        return nc.dram_tensor(name, shape, dtype, kind="Internal").ap()

    wT_d = {nm: scr(nm + "_T", [KD, 128, D], BF16)
            for nm in ("W_r", "W_k", "W_v", "W_g", "W_o", "Wgt")}
    cvT_d = scr("cvT_d", [3, KD, 128, FI0], BF16)
    egB_d = scr("egB_d", [NE0, KD, 128, FI0], BF16)
    euB_d = scr("euB_d", [NE0, KD, 128, FI0], BF16)
    edB_d = scr("edB_d", [NE0, KF, 128, D], BF16)
    suB_d = scr("suB_d", [KD, 128, FI0], BF16)
    sdB_d = scr("sdB_d", [KF, 128, D], BF16)
    wlamT_d = scr("wlamT_d", [KD, 128, H], BF16)
    wBT_d = scr("wBT_d", [KD, 128, H], BF16)
    wCT_d = scr("wCT_d", [H, D], BF16)
    s_d = scr("s_d", [SEQ, T, D], BF16)
    x1_d = scr("x1_d", [T, D], F32)
    xn1_d = scr("xn1_d", [T, D], BF16)
    s1_d = scr("s1_d", [T, D], BF16)
    opair_d = scr("opair_d", [E1 * PE_ + 1, D], F32)

    seqs = [SEQ_ORDER[i] for i in range(SEQ) if i < nseq]
    if nseq < SEQ:
        seqs = [s for s in SEQ_ORDER if s != 0][:max(0, nseq - 1)] + [0]

    with ExitStack() as st:
        p = Prog(nc, st)
        blk = st.enter_context(nc.Block())
        p.dbg_slot = [0]

        dbg_sel = os.environ.get("MK_DBGSEL", "")
        dbg_sel = ([int(x) for x in dbg_sel.split(",")] if dbg_sel
                   else None)

        def dbg_dump(ap):
            if not dbg_on or p.dbg_slot[0] >= 24:
                return
            slot = p.dbg_slot[0]
            p.dbg_slot[0] += 1
            if dbg_sel is not None and slot not in dbg_sel:
                return
            n = 1
            for d_ in ap.shape[1:]:
                n *= d_
            n = min(n, D)
            for eng in "tvsgy":
                for sem in ("PE", "DV", "AC", "GP"):
                    pass
            p.wait("g", "PE", p.cnt["PE"])
            p.wait("g", "DV", p.cnt["DV"])
            p.wait("g", "AC", p.cnt["AC"])
            p.wait("g", "GP", p.cnt["GP"])
            p.wait("g", "XL", p.cnt["XL"])
            p.wait("g", "WL", p.cnt["WL"])
            if len(ap.shape) == 3:
                flat = ap.rearrange("p a b -> p (a b)")
            else:
                flat = ap
            np_ = flat.shape[0]
            wlv = p.dma("g", dbg_h[slot, 0:np_, 0:n], flat[:, 0:n], "WL")
            for eng in "tvsgy":
                p.wait(eng, "WL", wlv)
        p.dbg_dump = dbg_dump

        mixc = {nm: p.sb("mx_" + nm, [128, KD], F32)
                for nm in ("r", "k", "v", "g")}
        ident = p.sb("ident", [128, 128], BF16)
        identf = p.sb("identf", [128, 128], F32)
        onesf = p.sb("onesf", [1, 128], F32)
        td_row = p.sb("td_row", [1, H], F32)
        u_row = p.sb("u_row", [1, H], F32)
        tdf_c = p.sb("tdf_c", [128, KD], F32)
        wsb = p.sb("wsb", [128, KD, HK], F32)
        wmatT = p.sb("wmatT", [128, H, CH], BF16)
        wb_f = p.sb("wb_f", [128, KD, CH], BF16)
        wkp_t = p.sb("wkp_t", [128, D], BF16)
        lnwb = p.sb("lnwb", [128, D], BF16)
        lnbb = p.sb("lnbb", [128, D], BF16)
        rtw_s = p.sb("rtw_s", [128, KD, 16], BF16)
        rtb_s = p.sb("rtb_s", [1, 16], F32)
        rowstg = p.sb("rowstg", [1, D], F32)
        sc_a = p.sb("sc_a", [128, 2 * CH], F32)
        sc_b = p.sb("sc_b", [128, H, CH], F32)
        cbias = p.sb("cbias", [128, 4], F32)
        sel16 = p.sb("sel16", [16, 16, 128], BF16)

        pst = st.enter_context(nc.psum_tensor("pst", [128, 8, 512], F32)).ap()

        def outer_rows(dst, row_ap, n, bankn=7, scale=None, dtypecast=None):
            done = []
            for c0 in range(0, n, 512):
                w = min(512, n - c0)
                p.bank(bankn)
                def f(e, c=c0, ww=w):
                    return e.matmul(pst[:, bankn, 0:ww], onesf[:, :],
                                    row_ap[:, c:c + ww], start=True, stop=True)
                pv = p.op("t", f, "PE")
                p.wait("v", "PE", pv)
                if scale is None:
                    def f(e, c=c0, ww=w):
                        return e.tensor_copy(dst[:, c:c + ww],
                                             pst[:, bankn, 0:ww])
                else:
                    def f(e, c=c0, ww=w):
                        return e.tensor_scalar(dst[:, c:c + ww],
                                               pst[:, bankn, 0:ww], scale,
                                               None, ALU.mult)
                ev = p.op("v", f, "DV")
                p.bank_done(bankn, "DV", ev)
                done.append(ev)
            return done[-1]

        def rsqrt_col(dst, src, scale, eps, postscale=1.0):
            ecol = {1e-5: 0, 1e-6: 1}[eps]
            bcol = 2 if postscale != 1.0 else 3
            def f1(e):
                return e.activation(dst, src, ACTF.Ln,
                                    bias=cbias[:, ecol:ecol + 1],
                                    scale=float(scale))
            p.op("s", f1, "AC")
            def f2(e):
                return e.activation(dst, dst, ACTF.Exp,
                                    bias=cbias[:, bcol:bcol + 1], scale=-0.5)
            return p.op("s", f2, "AC")

        for dst, src in ((mixc["r"], hb["mix_r"]), (mixc["k"], hb["mix_k"]),
                         (mixc["v"], hb["mix_v"]), (mixc["g"], hb["mix_g"]),
                         (td_row, hb["tdec"]), (u_row, hb["tfaa"]),
                         (rtb_s, hb["rt_b"])):
            p.dma("y", dst[:, :], src[:, :], "XL")
        sm_loaded = p.cnt["XL"]
        rtw_v = p.dma("g", rtw_s[:, :, :],
                      hb["rt_w"].rearrange("(k p) n -> p k n", p=128), "WL")

        def f(e):
            return e.memset(onesf[:, :], 1.0)
        ones_v = p.op("v", f, "DV")
        for col, val in ((0, 1e-5), (1, 1e-6), (2, float(np.log(0.125))),
                         (3, 0.0)):
            def f(e, c=col, v=val):
                return e.memset(cbias[:, c:c + 1], v)
            p.op("v", f, "DV")

        def f(e):
            return e.iota(sc_a[:, 0:CH], [[-1, CH]], base=0,
                          channel_multiplier=1,
                          allow_small_or_imprecise_dtypes=True)
        gv = p.op("g", f, "GP")
        p.wait("v", "GP", gv)
        def f(e):
            return e.tensor_scalar(ident[:, :], sc_a[:, 0:CH], 0.0, None,
                                   ALU.is_equal)
        p.op("v", f, "DV")
        def f(e):
            return e.tensor_scalar(identf[:, :], sc_a[:, 0:CH], 0.0, None,
                                   ALU.is_equal)
        ident_v = p.op("v", f, "DV")
        def f(e):
            return e.iota(sc_b[0:16, 0:16, :].bitcast(F32), [[-1, 16], [0, CH]],
                          base=0, channel_multiplier=1,
                          allow_small_or_imprecise_dtypes=True)
        gv2 = p.op("g", f, "GP")
        p.wait("v", "GP", gv2)
        def f(e):
            return e.tensor_scalar(sel16[:, :, :], sc_b[0:16, 0:16, :]
                                   .bitcast(F32), 0.0, None, ALU.is_equal)
        sel_v = p.op("v", f, "DV")

        if "P" in stages:
            _prep_stack = ExitStack()
            wrow_b = _prep_stack.enter_context(
                nc.sbuf_tensor("wrow_bP", [128, 3 * D], BF16)).ap()
            wtr_b = _prep_stack.enter_context(
                nc.sbuf_tensor("wtr_bP", [128, D], BF16)).ap()

            def prep_transpose(get_src, dst, n_out, n_in, ld_w=None,
                               tile_fn=None):
                if ld_w is None:
                    ld_w = n_in
                if tile_fn is None:
                    def tile_fn(ic):
                        return wrow_b[:, ic * 128:(ic + 1) * 128]
                for oc in range(n_out // 128):
                    lv = p.dma("g", wrow_b[:, 0:ld_w], get_src(oc), "WL")
                    evs = []
                    for ic in range(n_in // 128):
                        p.wait("t", "WL", lv)
                        p.wait("t", "DV", ident_v)
                        p.bank(7)
                        def f(e, icc=ic):
                            return e.transpose(
                                pst[:, 7, 0:64].bitcast(BF16),
                                tile_fn(icc), ident[:, :])
                        tv = p.op("t", f, "PE")
                        p.wait("v", "PE", tv)
                        def f(e, icc=ic):
                            return e.tensor_copy(
                                wtr_b[:, icc * 128:(icc + 1) * 128],
                                pst[:, 7, 0:64].bitcast(BF16))
                        ev = p.op("v", f, "DV")
                        p.bank_done(7, "DV", ev)
                        evs.append(ev)
                    p.wait("y", "DV", evs[-1])
                    stv = p.dma(
                        "y",
                        dst[:, :, oc * 128:(oc + 1) * 128].transpose([1, 0, 2]),
                        wtr_b[:, 0:n_in].rearrange("p (k c) -> p k c", c=128),
                        "XL")
                    p.wait("g", "DV", evs[-1])
                    p.wait("t", "XL", stv)

            for nm in ("W_r", "W_k", "W_v", "W_g", "W_o", "Wgt"):
                prep_transpose(
                    lambda oc, s=hb[nm]: s[oc * 128:(oc + 1) * 128, :],
                    wT_d[nm], D, D)
            cvw = wrow_b.rearrange("p (d t) -> p d t", t=3)
            for kk in range(3):
                prep_transpose(
                    lambda oc, s=hb["cv"]: s[oc * 128:(oc + 1) * 128, :, :]
                    .rearrange("p d t -> p (d t)"),
                    cvT_d[kk], FI0, D, ld_w=3 * D,
                    tile_fn=lambda ic, k=kk: cvw[:, ic * 128:(ic + 1) * 128, k])

            def f(e):
                return e.memset(wrow_b[:, 0:D], 0.0)
            mz = p.op("v", f, "DV")
            p.wait("g", "DV", mz)
            for src, dst in ((hb["Wlam"], wlamT_d), (hb["WB"], wBT_d)):
                lv = p.dma("g", wrow_b[0:H, 0:D], src[:, :], "WL")
                evs = []
                for ic in range(KD):
                    p.wait("t", "WL", lv)
                    p.wait("t", "DV", ident_v)
                    p.bank(7)
                    def f(e, icc=ic):
                        return e.transpose(
                            pst[:, 7, 0:64].bitcast(BF16),
                            wrow_b[:, icc * 128:(icc + 1) * 128],
                            ident[:, :])
                    tv = p.op("t", f, "PE")
                    p.wait("v", "PE", tv)
                    def f(e, icc=ic):
                        return e.tensor_copy(
                            wtr_b[:, icc * 16:(icc + 1) * 16],
                            pst[:, 7, 0:64].bitcast(BF16)[:, 0:16])
                    ev = p.op("v", f, "DV")
                    p.bank_done(7, "DV", ev)
                    evs.append(ev)
                p.wait("y", "DV", evs[-1])
                stv = p.dma("y", dst.transpose([1, 0, 2]),
                            wtr_b[:, 0:KD * 16].rearrange("p (k c) -> p k c",
                                                          c=16), "XL")
                p.wait("g", "DV", evs[-1])
                p.wait("t", "XL", stv)

            lv = p.dma("g", wrow_b[:, 0:KD * 16],
                       hb["WC"].rearrange("(k p) n -> p k n", p=128), "WL")
            evs = []
            for ic in range(KD):
                p.wait("t", "WL", lv)
                p.wait("t", "DV", ident_v)
                p.bank(7)
                def f(e, icc=ic):
                    return e.transpose(pst[0:16, 7, 0:64].bitcast(BF16),
                                       wrow_b[:, icc * 16:(icc + 1) * 16],
                                       ident[:, :])
                tv = p.op("t", f, "PE")
                p.wait("v", "PE", tv)
                def f(e, icc=ic):
                    return e.tensor_copy(
                        wtr_b[0:16, icc * 128:(icc + 1) * 128],
                        pst[0:16, 7, 0:64].bitcast(BF16))
                ev = p.op("v", f, "DV")
                p.bank_done(7, "DV", ev)
                evs.append(ev)
            p.wait("y", "DV", evs[-1])
            stv = p.dma("y", wCT_d[:, :], wtr_b[0:16, 0:D], "XL")
            p.wait("g", "DV", evs[-1])
            p.wait("t", "XL", stv)

            def cast_stream(src, dst, rows):
                for rc in range(rows // 128):
                    p.dma("g", dst[rc, :, :],
                          src[rc * 128:(rc + 1) * 128, :], "WL")
            for e_i in range(NE0):
                cast_stream(hb["eg"][e_i], egB_d[e_i], D)
                cast_stream(hb["eu"][e_i], euB_d[e_i], D)
                cast_stream(hb["ed"][e_i], edB_d[e_i], FI0)
            cast_stream(hb["su"], suB_d, D)
            cast_stream(hb["sd"], sdB_d, FI0)
            p.emit_all(blk)
            _prep_stack.close()
        prep_wl = p.cnt["WL"]
        prep_st = p.cnt["XL"]

        if "C" in stages:
            p.wait("s", "XL", sm_loaded)
            def f(e):
                return e.activation(td_row[:, :], td_row[:, :], ACTF.Exp)
            p.op("s", f, "AC")
            def f(e):
                return e.activation(td_row[:, :], td_row[:, :], ACTF.Copy,
                                    scale=-1.0)
            tdv = p.op("s", f, "AC")
            p.wait("v", "AC", tdv)

            def f(e):
                return e.tensor_copy(
                    rowstg.rearrange("a (h k) -> a h k", k=HK),
                    td_row[:, :].unsqueeze(2).to_broadcast([1, H, HK]))
            trv = p.op("v", f, "DV")
            p.wait("t", "DV", max(trv, ident_v))
            for ic in range(KD):
                p.bank(7)
                def f(e, icc=ic):
                    return e.matmul(pst[:, 7, 0:1],
                                    rowstg[0:1, icc * 128:(icc + 1) * 128],
                                    onesf[0:1, 0:1], start=True, stop=True)
                tv = p.op("t", f, "PE")
                p.wait("v", "PE", tv)
                def f(e, icc=ic):
                    return e.tensor_copy(tdf_c[:, icc:icc + 1],
                                         pst[:, 7, 0:1])
                ev = p.op("v", f, "DV")
                p.bank_done(7, "DV", ev)
            tdf_v = p.cnt["DV"]

            p.wait("s", "DV", tdf_v)
            def f(e):
                return e.activation(sc_a[:, 0:KD], tdf_c[:, :], ACTF.Exp,
                                    scale=float(CH))
            wsv = p.op("s", f, "AC")
            p.wait("v", "AC", wsv)
            def f(e):
                return e.tensor_copy(
                    wsb[:, :, :],
                    sc_a[:, 0:KD].unsqueeze(2).to_broadcast([128, KD, HK]))
            p.op("v", f, "DV")

            def f(e):
                return e.iota(sc_a[:, CH:2 * CH], [[1, CH]], base=0,
                              channel_multiplier=0,
                              allow_small_or_imprecise_dtypes=True)
            gv = p.op("g", f, "GP")
            p.wait("v", "GP", gv)
            for k in range(KD):
                def f(e, kk=k):
                    return e.tensor_scalar(sc_b[:, 2 * kk, :], sc_a[:, CH:2 * CH],
                                           tdf_c[:, kk:kk + 1], None, ALU.mult)
                p.op("v", f, "DV")
            p.wait("s", "DV", p.cnt["DV"])
            def f(e):
                return e.activation(wb_f[:, :, :],
                                    sc_b[:, 0:2 * KD:2, :], ACTF.Exp)
            wbfv = p.op("s", f, "AC")

            p.wait("g", "DV", p.cnt["DV"])
            def f(e):
                return e.iota(sc_a[:, 8:9], [[1, 1]], base=CH - 1,
                              channel_multiplier=-1,
                              allow_small_or_imprecise_dtypes=True)
            gv = p.op("g", f, "GP")
            p.wait("v", "AC", wbfv)
            tdrows_v = outer_rows(sc_b.rearrange("p h c -> p (h c)"),
                                  rowstg, D, bankn=6)
            p.wait("v", "GP", gv)
            p.wait("v", "DV", p.cnt["DV"])
            def f(e):
                return e.tensor_tensor(
                    sc_b.rearrange("p h c -> p (h c)")[:, 0:D],
                    sc_b.rearrange("p h c -> p (h c)")[:, 0:D],
                    sc_a[:, 8:9].to_broadcast([128, D]), ALU.mult)
            p.op("v", f, "DV")
            p.wait("s", "DV", p.cnt["DV"])
            def f(e):
                return e.activation(wkp_t[:, :],
                                    sc_b.rearrange("p h c -> p (h c)")[:, 0:D],
                                    ACTF.Exp)
            wkpv = p.op("s", f, "AC")

            p.wait("g", "DV", p.cnt["DV"])
            def f(e):
                return e.iota(sc_a[:, 0:CH], [[1, CH]], base=0,
                              channel_multiplier=-1,
                              allow_small_or_imprecise_dtypes=True)
            gv = p.op("g", f, "GP")
            p.wait("v", "GP", gv)
            p.wait("v", "AC", wkpv)
            def f(e):
                return e.tensor_scalar(sc_a[:, CH:2 * CH], sc_a[:, 0:CH],
                                       -1.0, 0.0, ALU.add, ALU.max)
            p.op("v", f, "DV")
            p.wait("v", "DV", p.cnt["DV"])
            def f(e):
                return e.tensor_scalar(sc_a[:, 0:CH], sc_a[:, 0:CH], 0.0,
                                       None, ALU.is_gt)
            p.op("v", f, "DV")
            p.wait("t", "DV", ones_v)
            p.wait("t", "AC", tdv)
            p.bank(6); p.bank(5)
            def f(e):
                return e.matmul(pst[:, 6, 0:H], onesf[:, :], td_row[:, :],
                                start=True, stop=True)
            p.op("t", f, "PE")
            def f(e):
                return e.matmul(pst[:, 5, 0:H], onesf[:, :], u_row[:, :],
                                start=True, stop=True)
            pu = p.op("t", f, "PE")
            p.wait("v", "PE", pu)
            def f(e):
                return e.tensor_tensor(
                    sc_b[:, :, :],
                    sc_a[:, CH:2 * CH].unsqueeze(1).to_broadcast([128, H, CH]),
                    pst[:, 6, 0:H].unsqueeze(2).to_broadcast([128, H, CH]),
                    ALU.mult)
            p.op("v", f, "DV")
            p.wait("s", "DV", p.cnt["DV"])
            def f(e):
                return e.activation(sc_b[:, :, :], sc_b[:, :, :], ACTF.Exp)
            ev2 = p.op("s", f, "AC")
            p.wait("v", "AC", ev2)
            p.wait("v", "DV", p.cnt["DV"])
            def f(e):
                return e.tensor_tensor(
                    sc_b[:, :, :], sc_b[:, :, :],
                    sc_a[:, 0:CH].unsqueeze(1).to_broadcast([128, H, CH]),
                    ALU.mult)
            p.op("v", f, "DV")
            p.wait("v", "DV", p.cnt["DV"])
            for h in range(H):
                def f(e, hh=h):
                    return e.scalar_tensor_tensor(
                        sc_b[:, hh, :], identf[:, :],
                        pst[:, 5, hh:hh + 1], sc_b[:, hh, :],
                        ALU.mult, ALU.add)
                p.op("v", f, "DV")
            p.wait("v", "DV", p.cnt["DV"])
            def f(e):
                return e.tensor_copy(wmatT[:, :, :], sc_b[:, :, :])
            wmv = p.op("v", f, "DV")
            p.bank_done(5, "DV", wmv)
            p.bank_done(6, "DV", wmv)

            p.wait("y", "PE", p.cnt["PE"])
            lwv = p.dma("y", rowstg[:, :], hb["ln_w"][:, :], "XL")
            p.wait("t", "XL", lwv)
            outer_rows(lnwb, rowstg, D, bankn=7)
            p.wait("y", "PE", p.cnt["PE"])
            lbv = p.dma("y", rowstg[:, :], hb["ln_b"][:, :], "XL")
            p.wait("t", "XL", lbv)
            outer_rows(lnbb, rowstg, D, bankn=7)

        if "L1" in stages and os.environ.get("MK_LIBEARLY", "1") == "1":
            p.wait("g", "GP", p.cnt["GP"])
            def f(e):
                return e.load_library(libcfg.mlp)
            p.op("g", f)
        consts_dv = p.cnt["DV"]
        consts_ac = p.cnt["AC"]
        p.emit_all(blk)

        ctx = dict(p=p, blk=blk, nc=nc, hb=hb, out_h=out_h, pst=pst,
                   ident=ident, identf=identf, onesf=onesf, mixc=mixc,
                   sel16=sel16, dbg_dump=p.dbg_dump,
                   tdf_c=tdf_c, wsb=wsb, wmatT=wmatT, wb_f=wb_f, wkp_t=wkp_t,
                   lnwb=lnwb, lnbb=lnbb, rtw_s=rtw_s, rtb_s=rtb_s,
                   rowstg=rowstg, sc_a=sc_a, sc_b=sc_b, st=st,
                   wT_d=wT_d, cvT_d=cvT_d, egB_d=egB_d, euB_d=euB_d,
                   edB_d=edB_d, suB_d=suB_d, sdB_d=sdB_d, wlamT_d=wlamT_d,
                   wBT_d=wBT_d, wCT_d=wCT_d, s_d=s_d, x1_d=x1_d,
                   xn1_d=xn1_d, s1_d=s1_d, opair_d=opair_d,
                   consts_dv=consts_dv, consts_ac=consts_ac,
                   prep_wl=prep_wl, prep_st=prep_st, plan=plan,
                   rtw_v=rtw_v,
                   outer_rows=outer_rows, rsqrt_col=rsqrt_col,
                   seqs=seqs, stages=stages)
        _build_main(ctx)

    if p.haz:
        for ln in p.haz.report():
            print(ln)
    mybir.codegen_inst_isa_subclasses(nc)
    return nc, hb




def _build_main(ctx):
    p = ctx["p"]
    blk, nc, hb, out_h, pst = (ctx[k] for k in
                               ("blk", "nc", "hb", "out_h", "pst"))
    ident, identf, onesf = ctx["ident"], ctx["identf"], ctx["onesf"]
    sel16 = ctx["sel16"]
    dbg_dump = ctx["dbg_dump"]
    mixc, wsb, wmatT = ctx["mixc"], ctx["wsb"], ctx["wmatT"]
    wb_f, wkp_t, lnwb, lnbb = ctx["wb_f"], ctx["wkp_t"], ctx["lnwb"], ctx["lnbb"]
    rtw_s, rtb_s, rowstg = ctx["rtw_s"], ctx["rtb_s"], ctx["rowstg"]
    sc_b = ctx["sc_b"]
    wT_d, cvT_d = ctx["wT_d"], ctx["cvT_d"]
    egB_d, euB_d, edB_d = ctx["egB_d"], ctx["euB_d"], ctx["edB_d"]
    suB_d, sdB_d = ctx["suB_d"], ctx["sdB_d"]
    wlamT_d, wBT_d, wCT_d = ctx["wlamT_d"], ctx["wBT_d"], ctx["wCT_d"]
    s_d, x1_d, xn1_d, s1_d, opair_d = (ctx[k] for k in
                                       ("s_d", "x1_d", "xn1_d", "s1_d",
                                        "opair_d"))
    plan, seqs, stages = ctx["plan"], ctx["seqs"], ctx["stages"]
    outer_rows, rsqrt_col = ctx["outer_rows"], ctx["rsqrt_col"]
    prep_st, rtw_v = ctx["prep_st"], ctx["rtw_v"]
    PE_, PT, KMAX, ZROW = plan.PE_, plan.PT, plan.KMAX, plan.ZROW
    NPG = PE_ // 16

    def barrier():
        for eng in "tvsgy":
            for sem in ("PE", "DV", "AC", "GP"):
                p.wait(eng, sem, p.cnt[sem])
            for sem in ("WL", "XL"):
                p.wait(eng, sem, p.cnt[sem])
        for b in range(8):
            p.bank_free[b] = ("DV", 0)

    def vsync():
        p.wait("v", "DV", p.cnt["DV"])

    def ssync():
        p.wait("s", "AC", p.cnt["AC"])

    def stage_A(s, a):
        xn_f = a["xn_f"]
        p.wait("y", "XL", prep_st)
        wv_ = {}
        for nm in ("W_r", "W_k", "W_v", "W_g", "W_o"):
            wv_[nm] = p.dma("y", a[nm][:, :, :],
                            wT_d[nm].transpose([1, 0, 2]), "XL")

        def f(e):
            return e.memset(xn_f[:, :, 0:2], 0.0)
        p.op("v", f, "DV")

        for ti in range(16):
            xb = a["xrow"]
            p.wait("y", "DV", p.cnt["DV"])
            lv = p.dma("y", xb[:, :], hb["x"][s, ti * 128:(ti + 1) * 128, :],
                       "XL")
            p.wait("v", "XL", lv)
            def f(e):
                return e.tensor_reduce(a["bns"][:, 0:1], xb[:, :], AXX,
                                       ALU.add)
            rv1 = p.op("v", f, "DV")
            p.wait("s", "DV", rv1)
            def f(e):
                return e.activation(a["bns"][:, 1:2], a["bns"][:, 0:1],
                                    ACTF.Copy, scale=1.0 / D)
            av1 = p.op("s", f, "AC")
            p.wait("v", "AC", av1)
            p.wait("v", "PE", p.cnt["PE"])
            def f(e):
                return e.tensor_scalar(a["xnt"][:, :], xb[:, :],
                                       a["bns"][:, 1:2], None, ALU.subtract)
            xv = p.op("v", f, "DV")
            p.wait("s", "DV", xv)
            def f(e):
                return e.activation(a["xnt"][:, :], a["xnt"][:, :],
                                    ACTF.Square)
            av2 = p.op("s", f, "AC")
            p.wait("v", "AC", av2)
            def f(e):
                return e.tensor_reduce(a["bns"][:, 2:3], a["xnt"][:, :], AXX,
                                       ALU.add)
            rv2 = p.op("v", f, "DV")
            p.wait("s", "DV", rv2)
            rsv = rsqrt_col(a["bns"][:, 3:4], a["bns"][:, 2:3], 1.0 / D, 1e-5)
            p.wait("v", "AC", rsv)
            def f(e):
                return e.tensor_scalar(a["xnt"][:, :], xb[:, :],
                                       a["bns"][:, 1:2], a["bns"][:, 3:4],
                                       ALU.subtract, ALU.mult)
            xnv = p.op("v", f, "DV")
            for k in range(KD):
                p.bank(7)
                p.wait("t", "DV", xnv)
                def f(e, kk=k):
                    return e.transpose(pst[:, 7, 0:64].bitcast(BF16),
                                       a["xnt"][:, kk * 128:(kk + 1) * 128],
                                       ident[:, :])
                tv = p.op("t", f, "PE")
                p.wait("v", "PE", tv)
                def f(e, kk=k, tii=ti):
                    return e.tensor_copy(
                        xn_f[:, kk, 2 + tii * 128:2 + (tii + 1) * 128],
                        pst[:, 7, 0:64].bitcast(BF16))
                ev = p.op("v", f, "DV")
                p.bank_done(7, "DV", ev)
        xn_ready = p.cnt["DV"]
        dbg_dump(xn_f[:, 0, 2:2 + 1024])
        dbg_dump(wmatT[:, 0, :])
        dbg_dump(wb_f[:, 0, :])
        dbg_dump(wkp_t[:, :])
        dbg_dump(lnwb[:, :])

        state_cp = {"v": 0}
        for q in range(4):
            t0 = q * 512

            def mk_mix(dst, coef, tq=None):
                tq = t0 if tq is None else tq
                p.wait("v", "PE", p.cnt["PE"])
                for k in range(KD):
                    def f(e, kk=k, ds=dst, tz=tq):
                        return e.tensor_tensor(
                            ds[:, kk, :],
                            xn_f[:, kk, 2 + tz:2 + tz + 512],
                            xn_f[:, kk, 1 + tz:1 + tz + 512], ALU.subtract)
                    p.op("v", f, "DV")
                p.wait("v", "DV", p.cnt["DV"])
                for k in range(KD):
                    def f(e, kk=k, ds=dst, cf=coef, tz=tq):
                        return e.scalar_tensor_tensor(
                            ds[:, kk, :], ds[:, kk, :], cf[:, kk:kk + 1],
                            xn_f[:, kk, 1 + tz:1 + tz + 512],
                            ALU.mult, ALU.add)
                    p.op("v", f, "DV")
                return p.cnt["DV"]

            def proj_f(dst, wbuf, wv_ready, mix, mix_v, banks=(0, 1)):
                p.wait("t", "XL", wv_ready)
                p.wait("t", "DV", mix_v)
                pv = 0
                for m in range(KD):
                    bk = banks[m % 2]
                    p.bank(bk)
                    for k in range(KD):
                        def f(e, kk=k, mm=m, b=bk, w=wbuf, mx=mix):
                            return e.matmul(
                                pst[:, b, :],
                                w[:, kk, mm * 128:(mm + 1) * 128],
                                mx[:, kk, :],
                                start=(kk == 0), stop=(kk == KD - 1))
                        if k == KD - 1:
                            pv = p.op("t", f, "PE")
                        else:
                            p.op("t", f)
                    p.wait("s", "PE", pv)
                    def f(e, mm=m, b=bk, ds=dst):
                        return e.activation(ds[:, mm, :], pst[:, b, :],
                                            ACTF.Copy)
                    av = p.op("s", f, "AC")
                    p.bank_done(bk, "AC", av)
                return p.cnt["AC"]

            mv = mk_mix(a["mix_q"], mixc["r"])
            r_ac = proj_f(a["rT"], a["W_r"], wv_["W_r"], a["mix_q"], mv)

            mv_g = mk_mix(a["mix_q2"], mixc["g"])
            p.wait("t", "XL", wv_["W_g"])
            p.wait("t", "DV", mv_g)
            for mt in range(4):
                for half in range(2):
                    bk = 2 + half
                    p.bank(bk)
                    for k in range(KD):
                        def f(e, kk=k, mtt=mt, hh=half, b=bk):
                            return e.matmul(
                                pst[:, b, :],
                                a["mix_q2"][:, kk, mtt * 128:(mtt + 1) * 128],
                                a["W_g"][:, kk, hh * 512:(hh + 1) * 512],
                                start=(kk == 0), stop=(kk == KD - 1))
                        if k == KD - 1:
                            pv = p.op("t", f, "PE")
                        else:
                            p.op("t", f)
                    p.wait("s", "PE", pv)
                    def f(e, mtt=mt, hh=half, b=bk):
                        return e.activation(
                            a["ggt_q"][:, mtt, hh * 512:(hh + 1) * 512],
                            pst[:, b, :], ACTF.Silu)
                    av = p.op("s", f, "AC")
                    p.bank_done(bk, "AC", av)

            mv_k = mk_mix(a["mix_q"], mixc["k"])
            k_ac = proj_f(a["kT"], a["W_k"], wv_["W_k"], a["mix_q"], mv_k)
            mv_v = mk_mix(a["mix_q2"], mixc["v"])

            for mt in range(4):
                for dst, wbuf, wv_r, mix, scale in (
                        (a["ktok_c"], a["W_k"], wv_["W_k"], a["mix_q"], True),
                        (a["vtok_c"], a["W_v"], wv_["W_v"], a["mix_q2"],
                         False)):
                    p.wait("t", "XL", wv_r)
                    p.wait("t", "DV", mv_v)
                    for half in range(2):
                        bk = 2 + half
                        p.bank(bk)
                        for k in range(KD):
                            def f(e, kk=k, hh=half, b=bk, mtt=mt, mx=mix,
                                  w=wbuf):
                                return e.matmul(
                                    pst[:, b, :],
                                    mx[:, kk, mtt * 128:(mtt + 1) * 128],
                                    w[:, kk, hh * 512:(hh + 1) * 512],
                                    start=(kk == 0), stop=(kk == KD - 1))
                            if k == KD - 1:
                                pv = p.op("t", f, "PE")
                            else:
                                p.op("t", f)
                        p.wait("v", "PE", pv)
                        if scale:
                            def f(e, hh=half, b=bk, ds=dst):
                                return e.tensor_tensor(
                                    ds[:, hh * 512:(hh + 1) * 512],
                                    pst[:, b, :],
                                    wkp_t[:, hh * 512:(hh + 1) * 512],
                                    ALU.mult)
                        else:
                            def f(e, hh=half, b=bk, ds=dst):
                                return e.tensor_copy(
                                    ds[:, hh * 512:(hh + 1) * 512],
                                    pst[:, b, :])
                        ev = p.op("v", f, "DV")
                        p.bank_done(bk, "DV", ev)
                kv_ready = p.cnt["DV"]

                p.wait("v", "AC", r_ac)
                p.wait("v", "PE", p.cnt["PE"])
                def f(e, mtt=mt):
                    return e.tensor_tensor(
                        a["rwb"][:, :, :],
                        a["rT"][:, :, mtt * 128:(mtt + 1) * 128],
                        wb_f[:, :, :], ALU.mult)
                rwb_v = p.op("v", f, "DV")

                p.wait("t", "AC", k_ac)
                p.wait("t", "DV", max(kv_ready, rwb_v))
                if state_cp["v"]:
                    p.wait("t", "DV", state_cp["v"])
                p.bank(2); p.bank(3); p.bank(4)
                attw_dv = {}
                for h in range(H):
                    hb_, hk_ = 64 * (h % 2), h // 2
                    bk = h % 2
                    if h >= 2:
                        p.wait("t", "DV", attw_dv[h - 2])
                    else:
                        p.bank(bk)
                    def f(e, b=bk, hb2=hb_, hk2=hk_, mtt=mt):
                        return e.matmul(
                            pst[:, b, 0:128],
                            a["kT"][hb2:hb2 + 64, hk2,
                                    mtt * 128:(mtt + 1) * 128],
                            a["rT"][hb2:hb2 + 64, hk2,
                                    mtt * 128:(mtt + 1) * 128],
                            start=True, stop=True)
                    apv = p.op("t", f, "PE")
                    p.wait("v", "PE", apv)
                    def f(e, hh=h, b=bk):
                        return e.tensor_tensor(a["attw"][:, hh % 2, :],
                                               pst[:, b, 0:128],
                                               wmatT[:, hh, :], ALU.mult)
                    attw_dv[h] = p.op("v", f, "DV")
                    p.wait("t", "DV", attw_dv[h])
                    xb_bank = 2 + (h // 8)
                    xoff = 64 * (h % 8)
                    def f(e, hh=h, b=xb_bank, xo=xoff):
                        return e.matmul(pst[:, b, xo:xo + 64],
                                        a["attw"][:, hh % 2, :],
                                        a["vtok_c"][:, 64 * hh:64 * hh + 64],
                                        start=(hh % 8 == 0), stop=False,
                                        skip_group_check=True)
                    p.op("t", f)
                    def f(e, hh=h, b=xb_bank, xo=xoff, hb2=hb_, hk2=hk_):
                        return e.matmul(pst[:, b, xo:xo + 64],
                                        a["rwb"][hb2:hb2 + 64, hk2, :],
                                        a["sbf"][hb2:hb2 + 64, hk2, :],
                                        start=False, stop=(hh % 8 == 7),
                                        skip_group_check=True)
                    p.op("t", f, "PE")
                    def f(e, hh=h, hb2=hb_, hk2=hk_):
                        return e.matmul(pst[hb2:hb2 + 64, 4,
                                            64 * hk2:64 * hk2 + 64],
                                        a["ktok_c"][:, 64 * hh:64 * hh + 64],
                                        a["vtok_c"][:, 64 * hh:64 * hh + 64],
                                        start=(hh <= 1), stop=(hh >= H - 2),
                                        skip_group_check=True)
                    p.op("t", f, "PE")
                xo_done = p.cnt["PE"]

                p.wait("v", "PE", xo_done)
                flat = a["s_state"][:, :, :].rearrange("p k v -> p (k v)")
                flatb = a["sbf"][:, :, :].rearrange("p k v -> p (k v)")
                wsbf = wsb[:, :, :].rearrange("p k v -> p (k v)")
                scf = sc_b.rearrange("p h c -> p (h c)")
                def f(e):
                    return e.tensor_tensor(scf[:, 0:512], flat, wsbf,
                                           ALU.mult)
                p.op("v", f, "DV")
                vsync()
                def f(e):
                    return e.tensor_tensor(flat, scf[:, 0:512], pst[:, 4, :],
                                           ALU.add)
                p.op("v", f, "DV")
                vsync()
                def f(e):
                    return e.tensor_copy(flatb, flat)
                state_cp["v"] = p.op("v", f, "DV")
                p.bank_done(4, "DV", state_cp["v"])

                p.wait("s", "PE", xo_done)
                p.wait("s", "DV", p.cnt["DV"])
                yv = 0
                for bb in range(2):
                    def f(e, b=2 + bb, o=512 * bb):
                        return e.activation(a["y_c"][:, o:o + 512],
                                            pst[:, b, :], ACTF.Copy)
                    yv = p.op("s", f, "AC")
                p.bank_done(2, "AC", yv)
                p.bank_done(3, "AC", yv)
                p.wait("v", "AC", yv)
                yv3 = a["y_c"].rearrange("p (h k) -> p h k", k=HK)
                def f(e):
                    return e.tensor_reduce(a["hn_sum"][:, :], yv3, AXX,
                                           ALU.add)
                p.op("v", f, "DV")
                def f(e):
                    return e.tensor_tensor(a["xnt"][:, :], a["y_c"][:, :],
                                           a["y_c"][:, :], ALU.mult)
                p.op("v", f, "DV")
                vsync()
                def f(e):
                    return e.tensor_reduce(
                        a["hn_sq"][:, :],
                        a["xnt"].rearrange("p (h k) -> p h k", k=HK), AXX,
                        ALU.add)
                p.op("v", f, "DV")
                def f(e):
                    return e.tensor_scalar(a["hn_mean"][:, :],
                                           a["hn_sum"][:, :], 1.0 / HK, None,
                                           ALU.mult)
                p.op("v", f, "DV")
                vsync()
                def f(e):
                    return e.tensor_tensor(a["hn_sum"][:, :],
                                           a["hn_mean"][:, :],
                                           a["hn_mean"][:, :], ALU.mult)
                p.op("v", f, "DV")
                vsync()
                def f(e):
                    return e.scalar_tensor_tensor(a["hn_var"][:, :],
                                                  a["hn_sq"][:, :], 1.0 / HK,
                                                  a["hn_sum"][:, :], ALU.mult,
                                                  ALU.subtract)
                vv = p.op("v", f, "DV")
                p.wait("s", "DV", vv)
                rsv = rsqrt_col(a["hn_rs"][:, :], a["hn_var"][:, :],
                                1.0 / HK, 1e-5, 0.125)
                p.wait("v", "AC", rsv)
                def f(e):
                    return e.tensor_tensor(
                        yv3, yv3, a["hn_mean"][:, :].unsqueeze(2)
                        .to_broadcast([128, H, HK]), ALU.subtract)
                p.op("v", f, "DV")
                vsync()
                def f(e):
                    return e.tensor_tensor(
                        yv3, yv3, a["hn_rs"][:, :].unsqueeze(2)
                        .to_broadcast([128, H, HK]), ALU.mult)
                p.op("v", f, "DV")
                vsync()
                def f(e):
                    return e.tensor_tensor(a["y_c"][:, :], a["y_c"][:, :],
                                           lnwb[:, :], ALU.mult)
                p.op("v", f, "DV")
                vsync()
                def f(e):
                    return e.tensor_tensor(a["y_c"][:, :], a["y_c"][:, :],
                                           lnbb[:, :], ALU.add)
                p.op("v", f, "DV")
                vsync()
                def f(e, mtt=mt):
                    return e.tensor_tensor(a["y_c"][:, :], a["y_c"][:, :],
                                           a["ggt_q"][:, mtt, :], ALU.mult)
                ygv = p.op("v", f, "DV")

                for k in range(KD):
                    bk = k % 2
                    p.bank(bk)
                    p.wait("t", "DV", ygv)
                    def f(e, kk=k, b=bk):
                        return e.transpose(
                            pst[:, b, 0:64].bitcast(BF16),
                            a["y_c"][:, kk * 128:(kk + 1) * 128], ident[:, :])
                    tv = p.op("t", f, "PE")
                    p.wait("v", "PE", tv)
                    def f(e, kk=k, b=bk):
                        return e.tensor_copy(a["yg_f"][:, kk, :],
                                             pst[:, b, 0:64].bitcast(BF16))
                    ev = p.op("v", f, "DV")
                    p.bank_done(bk, "DV", ev)
                ygf_v = p.cnt["DV"]
                p.wait("t", "XL", wv_["W_o"])
                p.wait("t", "DV", ygf_v)
                c = q * 4 + mt
                for half in range(2):
                    bk = 5 + half
                    p.bank(bk)
                    for k in range(KD):
                        def f(e, kk=k, hh=half, b=bk):
                            return e.matmul(
                                pst[:, b, :], a["yg_f"][:, kk, :],
                                a["W_o"][:, kk, hh * 512:(hh + 1) * 512],
                                start=(kk == 0), stop=(kk == KD - 1))
                        if k == KD - 1:
                            pv = p.op("t", f, "PE")
                        else:
                            p.op("t", f)
                    p.wait("s", "PE", pv)
                    p.wait("s", "XL", p.cnt["XL"])
                    def f(e, hh=half, b=bk):
                        return e.activation(
                            a["s_stage"][:, hh * 512:(hh + 1) * 512],
                            pst[:, b, :], ACTF.Copy)
                    av = p.op("s", f, "AC")
                    p.bank_done(bk, "AC", av)
                p.wait("y", "AC", p.cnt["AC"])
                p.dma("y", s_d[s, c * 128:(c + 1) * 128, :], a["s_stage"][:, :],
                      "XL")
                if c == 0:
                    dbg_dump(a["rT"][:, 0, :])
                    dbg_dump(a["ktok_c"][:, :])
                    dbg_dump(a["vtok_c"][:, :])
                    dbg_dump(a["y_c"][:, :])
                    dbg_dump(a["s_state"][:, 0, :])
                    dbg_dump(a["s_stage"][:, :])

    def stage_B(s, b, ln1=False):
        xn_f = b["xn_f"]
        p.wait("y", "XL", p.cnt["XL"])

        p.wait("t", "WL", rtw_v)
        prb = b["prb3"]
        for ti in range(16):
            bk = ti % 2
            p.bank(bk)
            for k in range(KD):
                def f(e, kk=k, b_=bk, tii=ti):
                    return e.matmul(
                        pst[:, b_, 0:16],
                        xn_f[:, kk, 2 + tii * 128:2 + (tii + 1) * 128],
                        rtw_s[:, kk, :], start=(kk == 0), stop=False)
                p.op("t", f)
            def f(e, b_=bk):
                return e.matmul(pst[:, b_, 0:16], onesf[:, :], rtb_s[:, :],
                                start=False, stop=True)
            pv = p.op("t", f, "PE")
            p.wait("v", "PE", pv)
            def f(e, b_=bk, tii=ti):
                return e.tensor_copy(prb[:, tii, :], pst[:, b_, 0:16])
            ev = p.op("v", f, "DV")
            p.bank_done(bk, "DV", ev)
        rA, rB, rC, rD = b["rA"], b["rB"], b["rC"], b["rD"]
        rE, rF = b["rE"], b["rF"]
        msk = b["msk3"]

        def bc(ap):
            return ap.unsqueeze(2).to_broadcast([128, 16, 16])
        vsync()
        p.op("v", lambda e: e.tensor_reduce(rA[:, :], prb, AXX, ALU.max),
             "DV")
        vsync()
        ev = p.op("v", lambda e: e.tensor_tensor(prb, prb, bc(rA),
                                                 ALU.subtract), "DV")
        p.wait("s", "DV", ev)
        av = p.op("s", lambda e: e.activation(
            prb.rearrange("p a b -> p (a b)"),
            prb.rearrange("p a b -> p (a b)"), ACTF.Exp), "AC")
        p.wait("v", "AC", av)
        p.op("v", lambda e: e.tensor_reduce(rA[:, :], prb, AXX, ALU.add),
             "DV")
        vsync()
        p.op("v", lambda e: e.reciprocal(rB[:, :], rA[:, :]), "DV")
        vsync()
        p.op("v", lambda e: e.tensor_tensor(prb, prb, bc(rB), ALU.mult),
             "DV")
        vsync()
        p.op("v", lambda e: e.tensor_reduce(rA[:, :], prb, AXX, ALU.max),
             "DV")
        vsync()
        p.op("v", lambda e: e.tensor_tensor(msk, prb, bc(rA), ALU.is_lt),
             "DV")
        vsync()
        p.op("v", lambda e: e.tensor_tensor(msk, msk, prb, ALU.mult), "DV")
        vsync()
        p.op("v", lambda e: e.tensor_reduce(rB[:, :], msk, AXX, ALU.max),
             "DV")
        vsync()
        p.op("v", lambda e: e.tensor_tensor(rC[:, :], rA[:, :], rB[:, :],
                                            ALU.add), "DV")
        vsync()
        p.op("v", lambda e: e.tensor_scalar(rC[:, :], rC[:, :], 1e-6, None,
                                            ALU.add), "DV")
        vsync()
        p.op("v", lambda e: e.reciprocal(rD[:, :], rC[:, :]), "DV")
        vsync()
        p.op("v", lambda e: e.tensor_tensor(rE[:, :], rA[:, :], rD[:, :],
                                            ALU.mult), "DV")
        p.op("v", lambda e: e.tensor_tensor(rF[:, :], rB[:, :], rD[:, :],
                                            ALU.mult), "DV")
        vsync()
        p.op("v", lambda e: e.tensor_tensor(msk, msk, bc(rB), ALU.is_equal),
             "DV")
        p.op("v", lambda e: e.tensor_tensor(prb, prb, bc(rA), ALU.is_equal),
             "DV")
        vsync()
        p.op("v", lambda e: e.tensor_tensor(msk, msk, bc(rF), ALU.mult),
             "DV")
        p.op("v", lambda e: e.tensor_tensor(prb, prb, bc(rE), ALU.mult),
             "DV")
        vsync()
        cv = p.op("v", lambda e: e.tensor_tensor(prb, prb, msk, ALU.add),
                  "DV")
        for ti in range(16):
            bk2 = 2 + ti % 2
            p.bank(bk2)
            p.wait("t", "DV", cv)
            def f(e, b_=bk2, tii=ti):
                return e.transpose(pst[0:16, b_, 0:128], prb[:, tii, :],
                                   identf[:, :])
            tv = p.op("t", f, "PE")
            p.wait("v", "PE", tv)
            def f(e, b_=bk2, tii=ti):
                return e.tensor_copy(
                    b["combT"][:, tii * 128:(tii + 1) * 128],
                    pst[0:16, b_, 0:128])
            ev = p.op("v", f, "DV")
            p.bank_done(bk2, "DV", ev)
        comb_v = p.cnt["DV"]
        dbg_dump(b["combT"][:, :])

        def make_wbc(e_i, t0):
            p.bank(6)
            p.wait("t", "DV", comb_v)
            def f(e, ei=e_i, tt=t0):
                return e.matmul(pst[:, 6, :], sel16[:, ei, :],
                                b["combT"][:, tt:tt + 512],
                                start=True, stop=True)
            pv = p.op("t", f, "PE")
            p.wait("v", "PE", pv)
            vsync()
            def f(e):
                return e.tensor_copy(b["wbc"][:, :], pst[:, 6, :])
            wv2 = p.op("v", f, "DV")
            p.bank_done(6, "DV", wv2)
            return wv2

        def expert_unit(gate_parts, up_buf, down_buf, w_ready, t0, wbc_e):
            use_wbc = wbc_e is not None
            wbcv = make_wbc(wbc_e, t0) if use_wbc else 0
            hp = b["hp"]
            n_parts = len(gate_parts)
            for m in range(KF):
                gb, ub = (m % 2) * 2, (m % 2) * 2 + 1
                p.bank(gb)
                p.bank(ub)
                p.wait("t", "XL", w_ready)
                upv = 0
                for gi, (gbuf, shift) in enumerate(gate_parts):
                    for k in range(KD):
                        st_ = (gi == 0 and k == 0)
                        sp_ = (gi == n_parts - 1 and k == KD - 1)
                        def f(e, kk=k, mm=m, b_=gb, g2=gbuf, sh=shift,
                              s1=st_, s2=sp_):
                            return e.matmul(
                                pst[:, b_, :],
                                g2[:, kk, mm * 128:(mm + 1) * 128],
                                xn_f[:, kk, 2 + t0 + sh:2 + t0 + sh + 512],
                                start=s1, stop=s2)
                        p.op("t", f)
                for k in range(KD):
                    def f(e, kk=k, mm=m, b_=ub):
                        return e.matmul(
                            pst[:, b_, :],
                            up_buf[:, kk, mm * 128:(mm + 1) * 128],
                            xn_f[:, kk, 2 + t0:2 + t0 + 512],
                            start=(kk == 0), stop=(kk == KD - 1))
                    if k == KD - 1:
                        upv = p.op("t", f, "PE")
                    else:
                        p.op("t", f)
                p.wait("s", "PE", upv)
                def f(e, b_=gb):
                    return e.activation(b["sg"][:, :], pst[:, b_, :],
                                        ACTF.Silu)
                av = p.op("s", f, "AC")
                p.bank_done(gb, "AC", av)
                p.wait("v", "AC", av)
                if use_wbc:
                    p.wait("v", "DV", wbcv)
                def f(e, mm=m, b_=ub):
                    return e.tensor_tensor(hp[:, mm, :], b["sg"][:, :],
                                           pst[:, b_, :], ALU.mult)
                hv = p.op("v", f, "DV")
                if use_wbc:
                    vsync()
                    def f(e, mm=m):
                        return e.tensor_tensor(hp[:, mm, :], hp[:, mm, :],
                                               b["wbc"][:, :], ALU.mult)
                    hv = p.op("v", f, "DV")
                p.bank_done(ub, "DV", hv)
            h_v = p.cnt["DV"]
            p.wait("t", "DV", h_v)
            racc_t = t0 % 1024
            for fo in range(KD):
                bk = 4 + fo % 2
                p.bank(bk)
                for k in range(KF):
                    def f(e, kk=k, ff=fo, b_=bk, w=down_buf):
                        return e.matmul(pst[:, b_, :],
                                        w[:, kk, ff * 128:(ff + 1) * 128],
                                        hp[:, kk, :],
                                        start=(kk == 0), stop=(kk == KF - 1))
                    if k == KF - 1:
                        pv = p.op("t", f, "PE")
                    else:
                        p.op("t", f)
                p.wait("v", "PE", pv)
                def f(e, ff=fo, b_=bk, rt=racc_t):
                    return e.tensor_tensor(
                        b["racc"][:, ff, rt:rt + 512],
                        b["racc"][:, ff, rt:rt + 512], pst[:, b_, :],
                        ALU.add)
                dv = p.op("v", f, "DV")
                p.bank_done(bk, "DV", dv)

        for hf in range(2):
            th0 = hf * 1024
            p.wait("v", "AC", p.cnt["AC"])
            for tt in range(2):
                t0 = th0 + tt * 512
                wv15 = make_wbc(15, t0)
                p.wait("v", "DV", wv15)
                for k in range(KD):
                    def f(e, kk=k, ts=tt * 512, tg=t0):
                        return e.tensor_tensor(
                            b["racc"][:, kk, ts:ts + 512],
                            xn_f[:, kk, 2 + tg:2 + tg + 512],
                            b["wbc"][:, :], ALU.mult)
                    p.op("v", f, "DV")

            with ExitStack() as shst:
                def sbl(n, shp, dt):
                    return shst.enter_context(
                        nc.sbuf_tensor(f"{n}_s{s}h{hf}", shp, dt)).ap()
                cvb = sbl("cvb", [128, 3, KD, FI0], BF16)
                sub = sbl("sub", [128, KD, FI0], BF16)
                sdb = sbl("sdb", [128, KF, D], BF16)
                barrier()
                lv1 = p.dma("y", cvb[:, :, :, :],
                            cvT_d.transpose([2, 0, 1, 3]), "XL")
                lv2 = p.dma("y", sub[:, :, :], suB_d.transpose([1, 0, 2]),
                            "XL")
                lv3 = p.dma("y", sdb[:, :, :], sdB_d.transpose([1, 0, 2]),
                            "XL")
                for tt in range(2):
                    t0 = th0 + tt * 512
                    expert_unit(
                        [(cvb[:, 0], -2), (cvb[:, 1], -1), (cvb[:, 2], 0)],
                        sub, sdb, max(lv1, lv2, lv3), t0, None)
                barrier()
                p.emit_all(blk)

            with ExitStack() as exst:
                def sbl(n, shp, dt):
                    return exst.enter_context(
                        nc.sbuf_tensor(f"{n}_e{s}h{hf}", shp, dt)).ap()
                egb = [sbl(f"egb{i}", [128, KD, FI0], BF16) for i in range(2)]
                eub = [sbl(f"eub{i}", [128, KD, FI0], BF16) for i in range(2)]
                edb = [sbl(f"edb{i}", [128, KF, D], BF16) for i in range(2)]
                barrier()
                last_use = [0, 0]
                w_ready = {}

                def load_e(e_i):
                    bf2 = e_i % 2
                    p.wait("y", "PE", last_use[bf2])
                    v1 = p.dma("y", egb[bf2][:, :, :],
                               egB_d[e_i].transpose([1, 0, 2]), "XL")
                    v2 = p.dma("y", eub[bf2][:, :, :],
                               euB_d[e_i].transpose([1, 0, 2]), "XL")
                    v3 = p.dma("y", edb[bf2][:, :, :],
                               edB_d[e_i].transpose([1, 0, 2]), "XL")
                    w_ready[e_i] = max(v1, v2, v3)

                load_e(0)
                for e_i in range(NE0):
                    bf2 = e_i % 2
                    if e_i + 1 < NE0:
                        load_e(e_i + 1)
                    for tt in range(2):
                        t0 = th0 + tt * 512
                        expert_unit([(egb[bf2], 0)], eub[bf2], edb[bf2],
                                    w_ready[e_i], t0, e_i)
                    last_use[bf2] = p.cnt["PE"]
                barrier()
                p.emit_all(blk)

            racc_v = p.cnt["DV"]
            if hf == 0:
                dbg_dump(b["racc"][:, 0, :])
                dbg_dump(b["hp"][:, :, :])
                dbg_dump(b["wbc"][:, :])
            for tt in range(2):
                t0 = th0 + tt * 512
                p.wait("s", "DV", racc_v)
                p.wait("s", "PE", p.cnt["PE"])
                gv = 0
                for k in range(KD):
                    def f(e, kk=k, ts=tt * 512):
                        return e.activation(b["gfb"][:, kk, :],
                                            b["racc"][:, kk, ts:ts + 512],
                                            ACTF.Silu)
                    gv = p.op("s", f, "AC")
                for mt in range(4):
                    c = (t0 // 128) + mt
                    vsync()
                    for k in range(KD):
                        bk = k % 2
                        p.bank(bk)
                        p.wait("t", "AC", gv)
                        def f(e, kk=k, b_=bk, mtt=mt):
                            return e.transpose(
                                pst[:, b_, 0:64].bitcast(BF16),
                                b["gfb"][:, kk, mtt * 128:(mtt + 1) * 128],
                                ident[:, :])
                        tv = p.op("t", f, "PE")
                        p.wait("v", "PE", tv)
                        def f(e, kk=k, b_=bk):
                            return e.tensor_copy(
                                b["g_t"][:, kk * 128:(kk + 1) * 128],
                                pst[:, b_, 0:64].bitcast(BF16))
                        ev = p.op("v", f, "DV")
                        p.bank_done(bk, "DV", ev)
                    gt_v = p.cnt["DV"]
                    p.wait("y", "DV", p.cnt["DV"])
                    sv = p.dma("y", b["s_t"][:, :],
                               s_d[s, c * 128:(c + 1) * 128, :], "XL")
                    xv = p.dma("y", b["x_t"][:, :],
                               hb["x"][s, c * 128:(c + 1) * 128, :], "XL")
                    p.wait("v", "XL", max(sv, xv))
                    p.wait("v", "DV", gt_v)
                    def f(e):
                        return e.tensor_tensor(b["g_t"][:, :], b["s_t"][:, :],
                                               b["g_t"][:, :], ALU.mult)
                    p.op("v", f, "DV")
                    p.wait("v", "DV", p.cnt["DV"])
                    def f(e):
                        return e.tensor_tensor(b["x_t"][:, :], b["x_t"][:, :],
                                               b["g_t"][:, :], ALU.add)
                    x1v = p.op("v", f, "DV")
                    if c == 0:
                        dbg_dump(b["s_t"][:, :])
                        dbg_dump(b["g_t"][:, :])
                        dbg_dump(b["x_t"][:, :])
                    p.wait("y", "DV", x1v)
                    if not ln1:
                        p.dma("y", out_h[s, c * 128:(c + 1) * 128, :],
                              b["x_t"][:, :], "XL")
                    else:
                        p.dma("y", x1_d[c * 128:(c + 1) * 128, :],
                              b["x_t"][:, :], "XL")
                        p.wait("v", "DV", p.cnt["DV"])
                        def f(e):
                            return e.tensor_reduce(b["bns"][:, 0:1],
                                                   b["x_t"][:, :], AXX,
                                                   ALU.add)
                        rv1 = p.op("v", f, "DV")
                        p.wait("s", "DV", rv1)
                        def f(e):
                            return e.activation(b["bns"][:, 1:2],
                                                b["bns"][:, 0:1],
                                                ACTF.Copy, scale=1.0 / D)
                        av1 = p.op("s", f, "AC")
                        p.wait("v", "AC", av1)
                        p.wait("v", "PE", p.cnt["PE"])
                        def f(e):
                            return e.tensor_scalar(
                                b["xn1_t"][:, :], b["x_t"][:, :],
                                b["bns"][:, 1:2], None, ALU.subtract)
                        xv2 = p.op("v", f, "DV")
                        p.wait("s", "DV", xv2)
                        def f(e):
                            return e.activation(b["xn1_t"][:, :],
                                                b["xn1_t"][:, :],
                                                ACTF.Square)
                        av2 = p.op("s", f, "AC")
                        p.wait("v", "AC", av2)
                        def f(e):
                            return e.tensor_reduce(b["bns"][:, 2:3],
                                                   b["xn1_t"][:, :], AXX,
                                                   ALU.add)
                        rv2 = p.op("v", f, "DV")
                        p.wait("s", "DV", rv2)
                        rsv = rsqrt_col(b["bns"][:, 3:4], b["bns"][:, 2:3],
                                        1.0 / D, 1e-5)
                        p.wait("v", "AC", rsv)
                        def f(e):
                            return e.tensor_scalar(
                                b["xn1_t"][:, :], b["x_t"][:, :],
                                b["bns"][:, 1:2], b["bns"][:, 3:4],
                                ALU.subtract, ALU.mult)
                        xnv = p.op("v", f, "DV")
                        p.wait("y", "DV", xnv)
                        p.dma("y", xn1_d[c * 128:(c + 1) * 128, :],
                              b["xn1_t"][:, :], "XL")
                        for k in range(KD):
                            bk = k % 2
                            p.bank(bk)
                            p.wait("t", "DV", xnv)
                            def f(e, kk=k, b_=bk):
                                return e.transpose(
                                    pst[:, b_, 0:64].bitcast(BF16),
                                    b["xn1_t"][:, kk * 128:(kk + 1) * 128],
                                    ident[:, :])
                            tv = p.op("t", f, "PE")
                            p.wait("v", "PE", tv)
                            def f(e, kk=k, b_=bk, cc=c):
                                return e.tensor_copy(
                                    b["xn1_f"][:, kk,
                                               cc * 128:(cc + 1) * 128],
                                    pst[:, b_, 0:64].bitcast(BF16))
                            ev = p.op("v", f, "DV")
                            p.bank_done(bk, "DV", ev)

    def stage_L1(xn1_f):
        with ExitStack() as ast:
            def sbl(n, shp, dt):
                return ast.enter_context(
                    nc.sbuf_tensor(n + "_l1a", shp, dt)).ap()
            wlam = sbl("wlam", [128, KD, H], BF16)
            wBb = sbl("wBb", [128, KD, H], BF16)
            wct = sbl("wct", [H, D], BF16)
            wgtb = sbl("wgtb", [128, KD, D], BF16)
            blam = sbl("blam_s", [H, 1], F32)
            lam_s = sbl("lam_s", [H, T], F32)
            bx_s = sbl("bx_s", [H, T], F32)
            hsb = sbl("hsb", [H, T], BF16)
            yg_s = sbl("yg_s", [128, D], F32)
            gsb = sbl("gsb", [128, 512], BF16)
            rmswb = sbl("rmswb", [128, D], BF16)
            s1t = sbl("s1t", [128, D], BF16)
            r_s = sbl("r_s", [128, 4], F32)
            sqs = sbl("sqs", [128, D], F32)
            barrier()
            l1 = p.dma("y", wlam[:, :, :], wlamT_d.transpose([1, 0, 2]), "XL")
            l2 = p.dma("y", wBb[:, :, :], wBT_d.transpose([1, 0, 2]), "XL")
            l3 = p.dma("y", wct[:, :], wCT_d[:, :], "XL")
            l4 = p.dma("y", wgtb[:, :, :], wT_d["Wgt"].transpose([1, 0, 2]),
                       "XL")
            l5 = p.dma("y", blam[:, :], hb["blam"][:, :], "XL")
            l6 = p.dma("y", rowstg[:, :], hb["rmsw"][:, :], "XL")
            p.wait("t", "XL", l6)
            outer_rows(rmswb, rowstg, D, bankn=7)
            for (wb_, dst, sig) in ((wlam, lam_s, True), (wBb, bx_s, False)):
                for tt in range(4):
                    bk = tt % 2
                    p.bank(bk)
                    p.wait("t", "XL", max(l1, l2, l5))
                    for k in range(KD):
                        def f(e, kk=k, b_=bk, ttt=tt, w=wb_):
                            return e.matmul(
                                pst[0:16, b_, :], w[:, kk, :],
                                xn1_f[:, kk, ttt * 512:(ttt + 1) * 512],
                                start=(kk == 0), stop=(kk == KD - 1))
                        if k == KD - 1:
                            pv = p.op("t", f, "PE")
                        else:
                            p.op("t", f)
                    p.wait("s", "PE", pv)
                    if sig:
                        def f(e, b_=bk, ttt=tt, ds=dst):
                            return e.activation(
                                ds[:, ttt * 512:(ttt + 1) * 512],
                                pst[0:16, b_, :], ACTF.Sigmoid,
                                bias=blam[:, 0:1])
                    else:
                        def f(e, b_=bk, ttt=tt, ds=dst):
                            return e.activation(
                                ds[:, ttt * 512:(ttt + 1) * 512],
                                pst[0:16, b_, :], ACTF.Copy)
                    av = p.op("s", f, "AC")
                    p.bank_done(bk, "AC", av)
            p.wait("v", "AC", p.cnt["AC"])
            def f(e):
                return e.tensor_tensor_scan(lam_s[:, :], lam_s[:, :],
                                            bx_s[:, :], 0.0, ALU.mult,
                                            ALU.add)
            p.op("v", f, "DV")
            vsync()
            def f(e):
                return e.tensor_copy(hsb[:, :], lam_s[:, :])
            hv = p.op("v", f, "DV")
            for sub in range(16):
                for half in range(2):
                    yb, gb = 2 + half, 4 + half
                    p.bank(yb)
                    p.bank(gb)
                    p.wait("t", "DV", hv)
                    p.wait("t", "XL", max(l3, l4))
                    def f(e, ss=sub, hh=half, b_=yb):
                        return e.matmul(pst[:, b_, :],
                                        hsb[:, ss * 128:(ss + 1) * 128],
                                        wct[:, hh * 512:(hh + 1) * 512],
                                        start=True, stop=True)
                    p.op("t", f, "PE")
                    for k in range(KD):
                        def f(e, kk=k, ss=sub, hh=half, b_=gb):
                            return e.matmul(
                                pst[:, b_, :],
                                xn1_f[:, kk, ss * 128:(ss + 1) * 128],
                                wgtb[:, kk, hh * 512:(hh + 1) * 512],
                                start=(kk == 0), stop=(kk == KD - 1))
                        if k == KD - 1:
                            pv = p.op("t", f, "PE")
                        else:
                            p.op("t", f)
                    p.wait("s", "PE", pv)
                    p.wait("s", "DV", p.cnt["DV"])
                    def f(e, b_=gb):
                        return e.activation(gsb[:, :], pst[:, b_, :],
                                            ACTF.Sigmoid)
                    av = p.op("s", f, "AC")
                    p.bank_done(gb, "AC", av)
                    p.wait("v", "AC", av)
                    def f(e, hh=half, b_=yb):
                        return e.tensor_tensor(
                            yg_s[:, hh * 512:(hh + 1) * 512], gsb[:, :],
                            pst[:, b_, :], ALU.mult)
                    ev = p.op("v", f, "DV")
                    p.bank_done(yb, "DV", ev)
                vsync()
                def f(e):
                    return e.tensor_tensor(sqs[:, :], yg_s[:, :], yg_s[:, :],
                                           ALU.mult)
                p.op("v", f, "DV")
                vsync()
                def f(e):
                    return e.tensor_reduce(r_s[:, 0:1], sqs[:, :], AXX,
                                           ALU.add)
                rv = p.op("v", f, "DV")
                p.wait("s", "DV", rv)
                rsv = rsqrt_col(r_s[:, 1:2], r_s[:, 0:1], 1.0 / D, 1e-6)
                p.wait("v", "AC", rsv)
                def f(e):
                    return e.tensor_scalar(yg_s[:, :], yg_s[:, :],
                                           r_s[:, 1:2], None, ALU.mult)
                p.op("v", f, "DV")
                vsync()
                p.wait("v", "XL", p.cnt["XL"])
                def f(e):
                    return e.tensor_tensor(s1t[:, :], yg_s[:, :],
                                           rmswb[:, :], ALU.mult)
                s1v = p.op("v", f, "DV")
                p.wait("y", "DV", s1v)
                p.dma("y", s1_d[sub * 128:(sub + 1) * 128, :], s1t[:, :],
                      "XL")
                if sub == 15:
                    dbg_dump(s1t[:, :])
            barrier()
            p.emit_all(blk)

        if os.environ.get("MK_L1CUT", "") == "1":
            return
        with ExitStack() as fst:
            def sbl(n, shp, dt):
                return fst.enter_context(
                    nc.sbuf_tensor(n + "_l1f", shp, dt)).ap()
            XgT = sbl("XgT", [128, KD, PE_], BF16)
            hT = sbl("hT", [128, KFF, PE_], BF16)
            w1b = [sbl(f"w1b{i}", [128, KD, 512], BF16) for i in range(2)]
            w2b = [sbl(f"w2b{i}", [128, KD, 512], BF16) for i in range(2)]
            b1s = sbl("b1s", [128, E1, KFF], F32)
            b2r = sbl("b2r", [1, D], F32)
            wvs = sbl("wvs", [128, E1, PT], F32)
            gix = sbl("gix", [128, E1, NPG], I16)
            opstb = sbl("opstb", [128, PT, D], F32)
            barrier()
            L1P = int(os.environ.get("MK_L1P", "7"))
            if L1P & 1 and os.environ.get("MK_LIBEARLY", "1") != "1":
                def f(e):
                    return e.load_library(libcfg.mlp)
                p.op("g", f)
            gz = zst = 0
            if not (L1P & 8):
                def f(e):
                    return e.memset(gix[:, :, :], 0)
                gz = p.op("v", f, "DV")
                def f(e):
                    return e.memset(b2r[:, :], 0.0)
                zv = p.op("v", f, "DV")
                p.wait("y", "DV", zv)
                zst = p.dma("y", opair_d[ZROW:ZROW + 1, :], b2r[:, :],
                            "XL")
            gv1 = 0
            if L1P & 2:
                p.wait("g", "DV", gz)
                gv1 = p.dma("g", gix[0:32, :, :], hb["l1gidx"][:, :, :],
                            "WL")
            bv = wvv = 0
            if L1P & 4:
                bv = p.dma("y", b1s[:, :, :], hb["b1l"][:, :, :], "XL")
                wvv = p.dma("y", wvs[:, :, :], hb["l1w"][:, :, :], "XL")
            xn1_wr = p.cnt["XL"]
            NT1 = [(i * 512, min(512, PE_ - i * 512))
                   for i in range((PE_ + 511) // 512)]
            w1_use = [0, 0]
            w2_use = [0, 0]
            for e_i in range(int(os.environ.get("MK_L1E", E1))):
                p.wait("g", "WL", gv1)
                p.wait("g", "XL", xn1_wr)
                p.wait("g", "PE", p.cnt["PE"])
                def f(e, ei=e_i):
                    return e.dma_gather(XgT[:, :, :], xn1_d[:, :],
                                        gix[:, ei, :], PE_, PE_, D,
                                        transpose=True)
                p.op("g", f, "WL", 16)
                gxv = p.dma("g", gix[0:1, 0:1, 0:8],
                            hb["l1gidx"][0:1, 0:1, 0:8], "WL")
                if e_i == 0:
                    p.wait("y", "XL", zst)
                b2v = p.dma("y", b2r[:, :], hb["b2"][e_i, :, :], "XL")
                for mg in range(KFF // 4):
                    wbuf = w1b[mg % 2]
                    p.wait("g", "PE", w1_use[mg % 2])
                    w1v = p.dma(
                        "g", wbuf[:, :, :],
                        hb["W1"][e_i][:, mg * 512:(mg + 1) * 512]
                        .rearrange("(k p) n -> p k n", p=128), "WL")
                    for m4 in range(4):
                        m = mg * 4 + m4
                        for (n0, nw) in NT1:
                            bk = 5 + (m4 % 2)
                            p.bank(bk)
                            p.wait("t", "WL", max(w1v, gxv))
                            for k in range(KD):
                                def f(e, kk=k, mm=m4, b_=bk, nn=n0, nwl=nw,
                                      w=wbuf):
                                    return e.matmul(
                                        pst[:, b_, 0:nwl],
                                        w[:, kk, mm * 128:(mm + 1) * 128],
                                        XgT[:, kk, nn:nn + nwl],
                                        start=(kk == 0), stop=(kk == KD - 1))
                                if k == KD - 1:
                                    pv = p.op("t", f, "PE")
                                else:
                                    p.op("t", f)
                            w1_use[mg % 2] = pv
                            p.wait("s", "PE", pv)
                            p.wait("s", "XL", bv)
                            def f(e, mm=m, b_=bk, nn=n0, nwl=nw, ei=e_i):
                                return e.activation(
                                    hT[:, mm, nn:nn + nwl],
                                    pst[:, b_, 0:nwl],
                                    ACTF.Gelu_apprx_tanh,
                                    bias=b1s[:, ei, mm:mm + 1])
                            av = p.op("s", f, "AC")
                            p.bank_done(bk, "AC", av)
                h_ac = p.cnt["AC"]
                if os.environ.get("MK_L1CUT", "") == "3":
                    continue
                p.wait("t", "AC", h_ac)
                for half in range(2):
                    for kg in range(4):
                        wbuf = w2b[kg % 2]
                        p.wait("g", "PE", w2_use[kg % 2])
                        w2v = p.dma(
                            "g", wbuf[:, :, :],
                            hb["W2"][e_i][kg * 1024:(kg + 1) * 1024,
                                          half * 512:(half + 1) * 512]
                            .rearrange("(k p) n -> p k n", p=128), "WL")
                        p.wait("t", "WL", w2v)
                        for pt in range(PT):
                            if kg == 0 and half == 0:
                                p.bank(pt)
                            for k in range(KD):
                                last = (pt == PT - 1 and k == KD - 1)
                                def f(e, kk=k, kgg=kg, b_=pt, w=wbuf):
                                    return e.matmul(
                                        pst[:, b_, :],
                                        hT[:, kgg * 8 + kk,
                                           b_ * 128:(b_ + 1) * 128],
                                        w[:, kk, :],
                                        start=(kgg == 0 and kk == 0),
                                        stop=False)
                                if last:
                                    w2_use[kg % 2] = p.op("t", f, "PE")
                                else:
                                    p.op("t", f)
                    p.wait("t", "XL", b2v)
                    for pt in range(PT):
                        def f(e, b_=pt, hh=half):
                            return e.matmul(pst[:, b_, :], onesf[:, :],
                                            b2r[:, hh * 512:(hh + 1) * 512],
                                            start=False, stop=True)
                        pv = p.op("t", f, "PE")
                        p.wait("v", "PE", pv)
                        p.wait("v", "XL", wvv)
                        def f(e, b_=pt, hh=half, ei=e_i):
                            return e.tensor_scalar(
                                opstb[:, b_, hh * 512:(hh + 1) * 512],
                                pst[:, b_, :], wvs[:, ei, b_:b_ + 1], None,
                                ALU.mult)
                        ev = p.op("v", f, "DV")
                        if half == 1:
                            p.bank_done(pt, "DV", ev)
                        else:
                            p.bank_done(pt, "DV", ev)
                ev_all = p.cnt["DV"]
                if e_i == 0:
                    dbg_dump(XgT[:, 0, :])
                    dbg_dump(hT[:, 0, :])
                    dbg_dump(opstb[:, 0, :])
                p.wait("y", "DV", ev_all)
                for pt in range(PT):
                    p.dma("y",
                          opair_d[e_i * PE_ + pt * 128:
                                  e_i * PE_ + (pt + 1) * 128, :],
                          opstb[:, pt, :], "XL")
                p.wait("v", "XL", p.cnt["XL"])
            barrier()
            p.emit_all(blk)

        if os.environ.get("MK_L1CUT", "") in ("2", "3"):
            return
        p.wait("y", "XL", p.cnt["XL"])
        with ExitStack() as cst:
            def sbl(n, shp, dt):
                return cst.enter_context(
                    nc.sbuf_tensor(n + "_l1c", shp, dt)).ap()
            acc = sbl("acc", [128, KD, D], F32)
            stag = [sbl(f"stag{i}", [128, KD, D], F32) for i in range(2)]
            cix = sbl("cix", [128, KMAX, 2, 64], I16)
            g1 = sbl("g1c", [128, D], BF16)
            s1c = sbl("s1c", [128, D], BF16)
            x1c = sbl("x1c", [128, D], F32)
            oc = sbl("oc", [128, D], F32)
            barrier()
            def f(e):
                return e.memset(cix[:, :, :, :], 0)
            cz = p.op("v", f, "DV")
            p.wait("g", "DV", cz)
            civ = p.dma("g", cix[0:32, :, :, :], hb["l1cidx"][:, :, :, :],
                        "WL")
            op_wr = p.cnt["XL"]
            for hf in range(2):
                p.wait("g", "WL", civ)
                p.wait("g", "XL", op_wr)
                p.wait("g", "AC", p.cnt["AC"])
                p.wait("g", "DV", p.cnt["DV"])
                def f(e, h=hf):
                    return e.dma_gather(acc[:, :, :], opair_d[:, :],
                                        cix[:, 0, h, :], 1024, 1024, D,
                                        transpose=False)
                p.op("g", f, "WL", 16)
                a0 = p.dma("g", cix[0:1, 0:1, 0:1, 0:8],
                           hb["l1cidx"][0:1, 0:1, 0:1, 0:8], "WL")
                add_v = 0
                for kk in range(1, KMAX):
                    sb_ = stag[kk % 2]
                    if add_v:
                        p.wait("g", "DV", add_v)
                    def f(e, h=hf, k2=kk, s2=sb_):
                        return e.dma_gather(s2[:, :, :], opair_d[:, :],
                                            cix[:, k2, h, :], 1024, 1024, D,
                                            transpose=False)
                    p.op("g", f, "WL", 16)
                    gvk = p.dma("g", cix[0:1, 0:1, 0:1, 0:8],
                                hb["l1cidx"][0:1, 0:1, 0:1, 0:8], "WL")
                    p.wait("v", "WL", gvk)
                    if kk == 1:
                        p.wait("v", "WL", a0)
                    def f(e, s2=sb_):
                        return e.tensor_tensor(
                            acc[:, :, :].rearrange("p k d -> p (k d)"),
                            acc[:, :, :].rearrange("p k d -> p (k d)"),
                            s2[:, :, :].rearrange("p k d -> p (k d)"),
                            ALU.add)
                    add_v = p.op("v", f, "DV")
                p.wait("s", "WL", p.cnt["WL"])
                p.wait("s", "DV", p.cnt["DV"])
                for tl in range(KD):
                    c = hf * 8 + tl
                    p.wait("s", "DV", p.cnt["DV"])
                    def f(e, t2=tl):
                        return e.activation(g1[:, :], acc[:, t2, :],
                                            ACTF.Silu)
                    gv2 = p.op("s", f, "AC")
                    p.wait("y", "DV", p.cnt["DV"])
                    sv = p.dma("y", s1c[:, :],
                               s1_d[c * 128:(c + 1) * 128, :], "XL")
                    xv = p.dma("y", x1c[:, :],
                               x1_d[c * 128:(c + 1) * 128, :], "XL")
                    p.wait("v", "AC", gv2)
                    p.wait("v", "XL", max(sv, xv))
                    if hf == 1 and c == 15:
                        dbg_dump(acc[:, 7, :])
                        dbg_dump(g1[:, :])
                    def f(e):
                        return e.tensor_tensor(g1[:, :], s1c[:, :], g1[:, :],
                                               ALU.mult)
                    p.op("v", f, "DV")
                    vsync()
                    def f(e):
                        return e.tensor_tensor(oc[:, :], x1c[:, :], g1[:, :],
                                               ALU.add)
                    ov = p.op("v", f, "DV")
                    p.wait("y", "DV", ov)
                    p.dma("y", out_h[0, c * 128:(c + 1) * 128, :], oc[:, :],
                          "XL")
            barrier()
            p.emit_all(blk)

    def seq_scope(s, ln1, run_l1=False):
        with ExitStack() as sst:
            xn_f = sst.enter_context(
                nc.sbuf_tensor(f"xn_f_q{s}", [128, KD, TPAD], BF16)).ap()
            xn1_f = None
            if "A" in stages:
                with ExitStack() as ast2:
                    def sba(n, shp, dt):
                        return ast2.enter_context(
                            nc.sbuf_tensor(f"{n}_a{s}", shp, dt)).ap()
                    a = dict(xn_f=xn_f)
                    for nm in ("W_r", "W_k", "W_v", "W_g", "W_o"):
                        a[nm] = sba("wb" + nm, [128, KD, D], BF16)
                    a["xrow"] = sba("xrow", [128, D], F32)
                    a["bns"] = sba("bns", [128, 4], F32)
                    a["xnt"] = sba("xnt", [128, D], BF16)
                    a["mix_q"] = sba("mix_q", [128, KD, 512], BF16)
                    a["mix_q2"] = sba("mix_q2", [128, KD, 512], BF16)
                    a["rT"] = sba("rT", [128, KD, 512], BF16)
                    a["kT"] = sba("kT", [128, KD, 512], BF16)
                    a["ggt_q"] = sba("ggt_q", [128, 4, D], BF16)
                    a["ktok_c"] = sba("ktok_c", [128, D], BF16)
                    a["vtok_c"] = sba("vtok_c", [128, D], BF16)
                    a["y_c"] = sba("y_c", [128, D], BF16)
                    a["rwb"] = sba("rwb", [128, KD, CH], BF16)
                    a["attw"] = sba("attw", [128, 2, CH], BF16)
                    a["s_state"] = sba("s_state", [128, KD, HK], F32)
                    a["sbf"] = sba("sbf", [128, KD, HK], BF16)
                    a["yg_f"] = sba("yg_f", [128, KD, CH], BF16)
                    a["s_stage"] = sba("s_stage", [128, D], BF16)
                    for nm in ("hn_sum", "hn_sq", "hn_mean", "hn_var",
                               "hn_rs"):
                        a[nm] = sba(nm, [128, H], F32)
                    barrier()
                    def f(e):
                        return e.memset(
                            a["s_state"].rearrange("p k v -> p (k v)"), 0.0)
                    p.op("v", f, "DV")
                    def f(e):
                        return e.memset(
                            a["sbf"].rearrange("p k v -> p (k v)"), 0.0)
                    p.op("v", f, "DV")
                    stage_A(s, a)
                    barrier()
                    p.emit_all(blk)
            if ln1:
                xn1_f = sst.enter_context(
                    nc.sbuf_tensor("xn1_f_g", [128, KD, T], BF16)).ap()
            if "B" in stages:
                with ExitStack() as bst:
                    def sbb(n, shp, dt):
                        return bst.enter_context(
                            nc.sbuf_tensor(f"{n}_b{s}", shp, dt)).ap()
                    b = dict(xn_f=xn_f, xn1_f=xn1_f)
                    b["racc"] = sbb("racc", [128, KD, 1024], F32)
                    b["gfb"] = sbb("gfb", [128, KD, 512], BF16)
                    b["hp"] = sbb("hp", [128, KF, 512], BF16)
                    b["combT"] = sbb("combT", [16, T], BF16)
                    b["wbc"] = sbb("wbc", [128, 512], BF16)
                    b["sg"] = sbb("sg", [128, 512], BF16)
                    b["prb3"] = sbb("prb3", [128, 16, 16], F32)
                    b["msk3"] = sbb("msk3", [128, 16, 16], F32)
                    for nm in ("rA", "rB", "rC", "rD", "rE", "rF"):
                        b[nm] = sbb(nm, [128, 16], F32)
                    b["s_t"] = sbb("s_t", [128, D], BF16)
                    b["x_t"] = sbb("x_t", [128, D], F32)
                    b["g_t"] = sbb("g_t", [128, D], BF16)
                    if ln1:
                        b["bns"] = sbb("bns1", [128, 4], F32)
                        b["xn1_t"] = sbb("xn1_t", [128, D], BF16)
                    barrier()
                    stage_B(s, b, ln1=ln1)
                    barrier()
                    p.emit_all(blk)
            if run_l1:
                stage_L1(xn1_f)
            p.emit_all(blk)

    if "A" in stages or "B" in stages:
        for s in seqs[:-1]:
            seq_scope(s, ln1=False)
        seq_scope(seqs[-1], ln1=("L1" in stages), run_l1=("L1" in stages))

    p.wait("y", "XL", p.cnt["XL"])
    p.wait("y", "WL", p.cnt["WL"])
    p.wait("g", "WL", p.cnt["WL"])
    p.wait("g", "XL", p.cnt["XL"])
    p.emit_all(blk)


def _prep_inputs(inputs, plan, core):
    f32 = np.float32
    x = np.asarray(inputs["L0_attn_xq"], f32)
    batches = [core + 8 * s for s in range(SEQ)]
    d = {"x": np.ascontiguousarray(x[batches])}

    def mixl(a):
        return np.ascontiguousarray(
            np.asarray(a, f32).reshape(KD, 128).T)
    d["mix_r"] = mixl(inputs["L0_attn_time_mix_r"])
    d["mix_k"] = mixl(inputs["L0_attn_time_mix_k"])
    d["mix_v"] = mixl(inputs["L0_attn_time_mix_v"])
    d["mix_g"] = mixl(inputs["L0_attn_time_mix_g"])
    d["tdec"] = np.asarray(inputs["L0_attn_time_decay"], f32).reshape(1, H)
    d["tfaa"] = np.asarray(inputs["L0_attn_time_faaaa"], f32).reshape(1, H)
    d["W_r"] = np.asarray(inputs["L0_attn_W_r"], f32)
    d["W_k"] = np.asarray(inputs["L0_attn_W_k"], f32)
    d["W_v"] = np.asarray(inputs["L0_attn_W_v"], f32)
    d["W_g"] = np.asarray(inputs["L0_attn_W_g"], f32)
    d["W_o"] = np.asarray(inputs["L0_attn_W_o"], f32)
    d["ln_w"] = np.asarray(inputs["L0_attn_ln_w"], f32).reshape(1, D)
    d["ln_b"] = np.asarray(inputs["L0_attn_ln_b"], f32).reshape(1, D)
    d["rt_w"] = np.asarray(inputs["L0_ffn_router_w"], f32)
    d["rt_b"] = np.asarray(inputs["L0_ffn_router_bias"], f32).reshape(1, 16)
    d["eg"] = np.asarray(inputs["L0_ffn_expert_gate_w"], f32)
    d["eu"] = np.asarray(inputs["L0_ffn_expert_up_w"], f32)
    d["ed"] = np.asarray(inputs["L0_ffn_expert_down_w"], f32)
    d["cv"] = np.ascontiguousarray(
        np.asarray(inputs["L0_ffn_conv_w"], f32))
    d["su"] = np.asarray(inputs["L0_ffn_shared_up_w"], f32)
    d["sd"] = np.asarray(inputs["L0_ffn_shared_down_w"], f32)
    d["Wlam"] = np.asarray(inputs["L1_attn_W_lambda"], f32)
    d["blam"] = np.asarray(inputs["L1_attn_b_lambda"], f32).reshape(H, 1)
    d["WB"] = np.asarray(inputs["L1_attn_W_B"], f32)
    d["WC"] = np.asarray(inputs["L1_attn_W_C"], f32)
    d["Wgt"] = np.asarray(inputs["L1_attn_W_gate"], f32)
    d["rmsw"] = np.asarray(inputs["L1_attn_rms_weight"], f32).reshape(1, D)
    d["W1"] = np.asarray(inputs["L1_ffn_W1"], f32)
    b1 = np.asarray(inputs["L1_ffn_b1"], f32)
    d["b1l"] = np.ascontiguousarray(
        b1.reshape(E1, KFF, 128).transpose(2, 0, 1))
    d["b2"] = np.asarray(inputs["L1_ffn_b2"], f32).reshape(E1, 1, D)
    d["W2"] = np.asarray(inputs["L1_ffn_W2"], f32)
    d["l1w"] = plan.wv[core]
    d["l1gidx"] = plan.gidx[core]
    d["l1cidx"] = plan.cidx[core]
    return d


def kernel(**inputs):
    plan = Plan(inputs["L1_ffn_top_idx"], inputs["L1_ffn_combine_weights"])
    nc, _ = build(plan)
    in_maps = [_prep_inputs(inputs, plan, core) for core in range(NCORE)]
    res = run_bass_kernel_spmd(nc, in_maps, core_ids=list(range(NCORE)))
    out = np.empty((B, T, D), np.float32)
    for core in range(NCORE):
        o = res.results[core]["out"]
        for s in range(SEQ):
            out[core + 8 * s] = o[s]
    return out
